# version 08
# baseline (speedup 1.0000x reference)
import contextlib
import math

import numpy as np
import ml_dtypes

import concourse.bass as bass
import concourse.mybir as mybir
from concourse.bass_utils import run_bass_kernel_spmd
from concourse.bass_isa import ReduceOp

F32 = mybir.dt.float32
BF16 = mybir.dt.bfloat16
U32 = mybir.dt.uint32
AF = mybir.ActivationFunctionType
ALU = mybir.AluOpType
AX = mybir.AxisListType

B, T, C = 4, 4096, 2048
H, HD, G, KV = 16, 128, 4, 512
FI = 4096
NT = 16
NSB = 32
TOK = NT * 128

LN127 = float(math.log(127.0))
GCONST = 1.0 / (2048.0 * 64.0 * 127.0 ** 3)
A1CLIP = 1e-5 * math.sqrt(2048.0)
WMEAN_N = float(C * FI)
FMASK = 0xFFFF8000

_CACHE = {}


def A(t, off, dims):
    return bass.AP(t.tensor if isinstance(t, bass.AP) else t, off, dims)


class Ctr:
    def __init__(self, sem):
        self.sem = sem
        self.v = 0

    def inc(self, inst, n=1):
        inst.then_inc(self.sem, n)
        self.v += n
        return self.v

    def dma(self, inst):
        return self.inc(inst, 16)


def build_program(taps=()):
    nc = bass.Bass("TRN2", target_bir_lowering=False,
               detect_race_conditions=False)
    dtn = nc.dram_tensor
    x_seq = dtn("x_seq", [T, C], F32, kind="ExternalInput").ap()
    x_own = dtn("x_own", [TOK, C], F32, kind="ExternalInput").ap()
    wq = dtn("wq", [C, C], F32, kind="ExternalInput").ap()
    bq = dtn("bq", [C], F32, kind="ExternalInput").ap()
    wk = dtn("wk", [C, KV], F32, kind="ExternalInput").ap()
    bk = dtn("bk", [KV], F32, kind="ExternalInput").ap()
    wv = dtn("wv", [C, KV], F32, kind="ExternalInput").ap()
    bv = dtn("bv", [KV], F32, kind="ExternalInput").ap()
    wo = dtn("wo", [C, C], F32, kind="ExternalInput").ap()
    bo = dtn("bo", [C], F32, kind="ExternalInput").ap()
    alpha = dtn("alpha", [H], F32, kind="ExternalInput").ap()
    wupt = dtn("wupt", [C, FI], F32, kind="ExternalInput").ap()
    wdnt = dtn("wdnt", [FI, C], F32, kind="ExternalInput").ap()
    cosr = dtn("cosr", [128, T], F32, kind="ExternalInput").ap()
    sinr = dtn("sinr", [128, T], F32, kind="ExternalInput").ap()
    qcos = dtn("qcos", [128, TOK], F32, kind="ExternalInput").ap()
    qsin = dtn("qsin", [128, TOK], F32, kind="ExternalInput").ap()
    emasks = dtn("emasks", [128, 8 * 512], BF16, kind="ExternalInput").ap()
    h128 = dtn("h128", [128, 128], BF16, kind="ExternalInput").ap()
    iden = dtn("iden", [128, 128], BF16, kind="ExternalInput").ap()
    hot16 = dtn("hot16", [128, 256], BF16, kind="ExternalInput").ap()
    hot4 = dtn("hot4", [128, 16], BF16, kind="ExternalInput").ap()
    rsel4 = dtn("rsel4", [4, 512], F32, kind="ExternalInput").ap()
    rsel16 = dtn("rsel16", [16, 2048], F32, kind="ExternalInput").ap()

    out_own = dtn("out_own", [TOK, C], F32, kind="ExternalOutput").ap()
    x2_scr = dtn("x2_scr", [TOK, C], F32, kind="Internal").ap()
    h_scr = dtn("h_scr", [FI, TOK], BF16, kind="Internal").ap()
    wdn_scr = dtn("wdn_scr", [FI, C], BF16, kind="Internal").ap()
    y_scr = dtn("y_scr", [128, H * TOK], BF16, kind="Internal").ap()
    g_scr = dtn("g_scr", [2048], F32, kind="Internal").ap()

    tap_aps = {}
    for tname, tshape, tdtt in taps:
        tap_aps[tname] = dtn("tap_" + tname, tshape, tdtt,
                             kind="ExternalOutput").ap()

    ctx = contextlib.ExitStack()
    E = ctx.enter_context
    with ctx:
        s_const = E(nc.sbuf_tensor("s_const", [128, 8], F32))
        CVALS = [1.0, -1.0, 0.5, -0.5, 1e-6, LN127, 1e-5]
        for j, v in enumerate(CVALS):
            nc.const_aps.register(F32, v, s_const[:, j:j + 1])
        hadm = E(nc.sbuf_tensor("hadm", [128, 128], BF16))
        idn = E(nc.sbuf_tensor("idn", [128, 128], BF16))
        hotk = E(nc.sbuf_tensor("hotk", [128, 16], BF16))
        ones1f = E(nc.sbuf_tensor("ones1f", [1, 128], F32))
        onescf = E(nc.sbuf_tensor("onescf", [128, 1], F32))
        onescb = E(nc.sbuf_tensor("onescb", [128, 1], BF16))
        bqf = E(nc.sbuf_tensor("bqf", [128, 16], F32))
        bkf = E(nc.sbuf_tensor("bkf", [128, 4], F32))
        bvb = E(nc.sbuf_tensor("bvb", [128, KV], F32))
        al16 = E(nc.sbuf_tensor("al16", [16, 1], F32))
        al1r = E(nc.sbuf_tensor("al1r", [1, 16], F32))
        negc = E(nc.sbuf_tensor("negc", [128, 1], F32))

        rts = E(nc.sbuf_tensor("rts", [128, 128], F32))
        tsc = E(nc.sbuf_tensor("tsc", [1, 16], F32))
        thr = E(nc.sbuf_tensor("thr", [128, 6], F32))

        sem_d = Ctr(E(nc.semaphore("sem_d")))
        sem_sp = Ctr(E(nc.semaphore("sem_sp")))
        sem_o = Ctr(E(nc.semaphore("sem_o")))
        S = {}
        for nm in ["s_ap", "t_tr", "v_ef", "t_vm", "v_ve", "s_ke",
                   "v_kq", "t_ks", "v_kg", "s_kr", "t_kb", "v_kn", "t_km",
                   "v_st", "s_rs", "v_nm",
                   "v_wc", "t_qm", "s_qe", "v_qq", "t_ss", "v_rp",
                   "s_rm", "t_b2", "v_s2", "v_sg",
                   "t_sc", "t_av", "t_dn", "t_b", "s_ex", "s_dr", "v_mk",
                   "v_ye", "v_sc",
                   "t_op", "v_x2", "s_l2", "t_t2", "v_lf", "t_wh",
                   "v_w1", "v_mx", "t_mt", "v_a1", "s_sc", "t_zt", "v_q1",
                   "v_zb",
                   "v_t6", "t_u6", "s_r6", "v_h6",
                   "t_z7", "v_z7", "v_mx7", "t_mt7", "v_a2", "s_sc7",
                   "t_zt7", "v_q2", "v_zb7", "v_t7", "t_dm", "s_de", "v_oe"]:
            S[nm] = Ctr(E(nc.semaphore(nm)))

        def snap():
            return {k: S[k].v for k in S}

        def ln_vec(v, xs, dsc, lnt, j, i, base, dtok,
                   nm_vst="v_st", nm_srs="s_rs", nm_vnm="v_nm", nbuf=2):
            v.wait_ge(sem_d.sem, dtok)
            if i >= nbuf:
                v.wait_ge(S[nm_srs].sem, base[nm_srs] + i - nbuf + 1)
            v.tensor_scalar(dsc[:], xs, 0.0, 0.0, ALU.add, ALU.add,
                            accum_out=lnt[:, j * 6 + 5:j * 6 + 6])
            v.scalar_tensor_tensor(dsc[:], xs, 1.0, xs, ALU.mult, ALU.mult,
                                   accum_out=lnt[:, j * 6 + 4:j * 6 + 5])
            v.tensor_scalar(lnt[:, j * 6 + 0:j * 6 + 1],
                            lnt[:, j * 6 + 5:j * 6 + 6], 1.0 / 2048.0, None,
                            ALU.mult)
            v.tensor_scalar(lnt[:, j * 6 + 4:j * 6 + 5],
                            lnt[:, j * 6 + 4:j * 6 + 5], 1.0 / 2048.0, None,
                            ALU.mult)
            v.tensor_mul(lnt[:, j * 6 + 5:j * 6 + 6],
                         lnt[:, j * 6 + 0:j * 6 + 1],
                         lnt[:, j * 6 + 0:j * 6 + 1])
            v.tensor_sub(lnt[:, j * 6 + 4:j * 6 + 5],
                         lnt[:, j * 6 + 4:j * 6 + 5],
                         lnt[:, j * 6 + 5:j * 6 + 6])
            S[nm_vst].inc(v.tensor_scalar(
                lnt[:, j * 6 + 4:j * 6 + 5], lnt[:, j * 6 + 4:j * 6 + 5],
                1e-5, None, ALU.add))
            v.wait_ge(S[nm_srs].sem, base[nm_srs] + i + 1)
            S[nm_vnm].inc(v.scalar_tensor_tensor(
                lnt[:, j * 6 + 2:j * 6 + 3], lnt[:, j * 6 + 0:j * 6 + 1],
                -1.0, lnt[:, j * 6 + 1:j * 6 + 2], ALU.mult, ALU.mult))

        def ln_scalar(s, xs, lnout, lnt, j, i, base,
                      nm_vst="v_st", nm_srs="s_rs", nm_vnm="v_nm",
                      nm_sap="s_ap"):
            s.wait_ge(S[nm_vst].sem, base[nm_vst] + i + 1)
            s.activation(lnt[:, j * 6 + 3:j * 6 + 4],
                         lnt[:, j * 6 + 4:j * 6 + 5], AF.Ln)
            S[nm_srs].inc(s.activation(
                lnt[:, j * 6 + 1:j * 6 + 2], lnt[:, j * 6 + 3:j * 6 + 4],
                AF.Exp, scale=-0.5))
            s.wait_ge(S[nm_vnm].sem, base[nm_vnm] + i + 1)
            S[nm_sap].inc(s.activation(
                lnout, xs, AF.Identity, bias=lnt[:, j * 6 + 2:j * 6 + 3],
                scale=lnt[:, j * 6 + 1:j * 6 + 2]))

        bvr = E(nc.sbuf_tensor("bvr", [1, KV], F32))
        with nc.Block() as blk:
            @blk.sync
            def _(sync):
                for dst, src in [
                    (hadm[:], h128), (idn[:], iden),
                    (hotk[:], hot4),
                    (bqf[:], A(bq, 0, [[1, 128], [128, 16]])),
                    (bkf[:], A(bk, 0, [[1, 128], [128, 4]])),
                    (al16[:, 0:1], alpha), (al1r[0:1, :], alpha),
                    (bvr[0:1, :], bv),
                ]:
                    sem_d.dma(sync.dma_start(out=dst, in_=src,
                                             allow_slow_non_contiguous=True))
                sync.wait_ge(sem_d.sem, sem_d.v)

        with nc.Block() as blk:
            @blk.vector
            def _(v):
                for j, val in enumerate(CVALS):
                    v.memset(s_const[:, j:j + 1], val)
                v.memset(ones1f[:], 1.0)
                v.memset(onescf[:], 1.0)
                v.memset(onescb[:], 1.0)
                v.memset(rts[:], 0.0)
                v.memset(tsc[:], 0.0)
                v.tensor_reduce(tsc[0:1, 0:1], al1r[:], AX.X, ALU.max,
                                apply_absolute_value=True)

        p0ctx = contextlib.ExitStack()
        p0b = p0ctx.enter_context(nc.psum_tensor("p0b", [128, 512], F32))
        with nc.Block() as blk:
            @blk.tensor
            def _(t):
                t.matmul(p0b[:, 0:KV], ones1f[:], bvr[:], start=True, stop=True)
        with nc.Block() as blk:
            @blk.vector
            def _(v):
                v.tensor_copy(bvb[:], p0b[:, 0:KV])
                v.tensor_scalar(tsc[0:1, 1:2], tsc[0:1, 0:1], -1.0, None,
                                ALU.mult)
        with nc.Block() as blk:
            @blk.tensor
            def _(t):
                t.matmul(p0b[:, 0:1], ones1f[:], tsc[0:1, 1:2],
                         start=True, stop=True)
        with nc.Block() as blk:
            @blk.vector
            def _(v):
                v.tensor_copy(negc[:], p0b[:, 0:1])
        p0ctx.close()

        attn_ctx = contextlib.ExitStack()
        EA = attn_ctx.enter_context
        kfm = EA(nc.sbuf_tensor("kfm", [128, G * T], BF16))
        vtm = EA(nc.sbuf_tensor("vtm", [128, G * T], BF16))

        p1 = contextlib.ExitStack()
        E1 = p1.enter_context
        rs4 = E1(nc.sbuf_tensor("rs4", [4, 512], F32))
        ctab = E1(nc.sbuf_tensor("ctab", [128, T], F32))
        stab = E1(nc.sbuf_tensor("stab", [128, T], F32))
        wkb = E1(nc.sbuf_tensor("wkb", [128, 16 * KV], BF16))
        wvb = E1(nc.sbuf_tensor("wvb", [128, 16 * KV], BF16))
        wst1 = E1(nc.sbuf_tensor("wst1", [128, 2048], F32))
        with nc.Block() as blk:
            @blk.sync
            def _(sync):
                sem_d.dma(sync.dma_start(out=ctab[:], in_=cosr))
                sem_d.dma(sync.dma_start(out=stab[:], in_=sinr))
                sem_d.dma(sync.dma_start(out=rs4[:], in_=rsel4))
                sync.wait_ge(sem_d.sem, sem_d.v)
        for wi in range(8):
            wsrc = wk if wi < 4 else wv
            wdst = wkb if wi < 4 else wvb
            qu = wi % 4
            with nc.Block() as blk:
                @blk.sync
                def _(sync, wsrc=wsrc, qu=qu):
                    src = A(wsrc, qu * 4 * 128 * KV,
                            [[KV, 128], [128 * KV, 4], [1, KV]])
                    sem_d.dma(sync.dma_start(out=wst1[:], in_=src))
                    sync.wait_ge(sem_d.sem, sem_d.v)
            with nc.Block() as blk:
                @blk.vector
                def _(v, wdst=wdst, qu=qu):
                    v.tensor_copy(wdst[:, qu * 4 * KV:(qu + 1) * 4 * KV],
                                  wst1[:])

        xb = E1(nc.sbuf_tensor("xb", [128, 2 * 2048], F32))
        lntm = E1(nc.sbuf_tensor("lntm", [128, 2048], BF16))
        lnfm4 = E1(nc.sbuf_tensor("lnfm4", [128, 16 * 512], BF16))
        dsc1 = E1(nc.sbuf_tensor("dsc1", [128, 2048], BF16))
        lnt1 = E1(nc.sbuf_tensor("lnt1", [128, 2 * 6], F32))
        ksb = E1(nc.sbuf_tensor("ksb", [128, 4 * 512], BF16))
        ksq = E1(nc.sbuf_tensor("ksq", [128, 2 * 512], BF16))
        kssf = E1(nc.sbuf_tensor("kssf", [4, 512], F32))
        fk = E1(nc.sbuf_tensor("fk", [4, 512], F32))
        knb = E1(nc.sbuf_tensor("knb", [128, 512], BF16))
        kswb = E1(nc.sbuf_tensor("kswb", [128, 512], BF16))
        krt = E1(nc.sbuf_tensor("krt", [128, 512], F32))
        pt_ln = E1(nc.psum_tensor("pt_ln", [128, 2048], BF16))
        pk = E1(nc.psum_tensor("pk", [128, 2 * 512], F32))
        pv = E1(nc.psum_tensor("pv", [128, 2 * 512], F32))
        pkss = E1(nc.psum_tensor("pkss", [4, 512], F32))
        pbc1 = E1(nc.psum_tensor("pbc1", [128, 512], F32))

        bs = snap()
        bd = sem_d.v

        with nc.Block() as blk:
            @blk.sync
            def _(sync, bs=bs):
                for sb in range(NSB):
                    if sb >= 2:
                        sync.wait_ge(S["s_ap"].sem, bs["s_ap"] + sb - 1)
                    sem_d.dma(sync.dma_start(
                        out=xb[:, (sb % 2) * 2048:(sb % 2 + 1) * 2048],
                        in_=x_seq[sb * 128:(sb + 1) * 128, :]))

            @blk.vector
            def _(v, bs=bs, bd=bd):
                for gi in range(8):
                    for k in range(4):
                        sb = 4 * gi + k
                        j = sb % 2
                        xs = xb[:, j * 2048:(j + 1) * 2048]
                        ln_vec(v, xs, dsc1, lnt1, j, sb, bs,
                               bd + 16 * (sb + 1))
                        v.wait_ge(S["t_tr"].sem, bs["t_tr"] + sb + 1)
                        if gi > 0:
                            v.wait_ge(S["t_km"].sem, bs["t_km"] + 4 * gi)
                            v.wait_ge(S["t_vm"].sem, bs["t_vm"] + 4 * gi)
                        S["v_ef"].inc(v.tensor_copy(
                            A(lnfm4, k * 128, [[16 * 512, 128], [512, 16],
                                               [1, 128]]),
                            A(pt_ln, 0, [[2048, 128], [128, 16],
                                         [1, 128]])))
                    for k in range(4):
                        sb = 4 * gi + k
                        j = sb % 2
                        v.wait_ge(S["t_vm"].sem, bs["t_vm"] + sb + 1)
                        S["v_ve"].inc(v.tensor_tensor(
                            A(vtm, sb * 128, [[G * T, 128], [T, 4], [1, 128]]),
                            A(pv, j * 512, [[2 * 512, 128], [128, 4],
                                            [1, 128]]),
                            A(bvb, 0, [[KV, 128], [128, 4], [1, 128]]),
                            ALU.add))
                    for g in range(4):
                        idx = 4 * gi + g
                        j = idx % 2
                        v.wait_ge(S["s_ke"].sem, bs["s_ke"] + idx + 1)
                        if idx >= 2:
                            v.wait_ge(S["t_ks"].sem, bs["t_ks"] + idx - 1)
                        S["v_kq"].inc(v.tensor_mul(
                            ksq[:, j * 512:(j + 1) * 512],
                            ksb[:, g * 512:(g + 1) * 512],
                            ksb[:, g * 512:(g + 1) * 512]))
                    v.wait_ge(S["t_ks"].sem, bs["t_ks"] + 4 * gi + 4)
                    S["v_kg"].inc(v.tensor_scalar(kssf[:], pkss[:], 1e-30,
                                                  None, ALU.max))
                    for g in range(4):
                        idx = 4 * gi + g
                        v.wait_ge(S["t_kb"].sem, bs["t_kb"] + idx + 1)
                        v.tensor_mul(knb[:], ksb[:, g * 512:(g + 1) * 512],
                                     pbc1[:])
                        v.tensor_copy(kswb[0:64, :], knb[64:128, :])
                        v.tensor_copy(kswb[64:128, :], knb[0:64, :])
                        s0 = 4 * gi * 128
                        v.tensor_mul(krt[:], knb[:], ctab[:, s0:s0 + 512])
                        v.tensor_mul(kswb[:], kswb[:], stab[:, s0:s0 + 512])
                        S["v_kn"].inc(v.tensor_add(
                            kfm[:, g * T + s0:g * T + s0 + 512],
                            krt[:], kswb[:]))

            @blk.scalar
            def _(s, bs=bs):
                for gi in range(8):
                    for k in range(4):
                        sb = 4 * gi + k
                        j = sb % 2
                        if sb >= 1:
                            s.wait_ge(S["t_tr"].sem, bs["t_tr"] + sb)
                        ln_scalar(s, xb[:, j * 2048:(j + 1) * 2048],
                                  lntm[:], lnt1, j, sb, bs)
                    for g in range(4):
                        idx = 4 * gi + g
                        j = idx % 2
                        s.wait_ge(S["t_km"].sem, bs["t_km"] + idx + 1)
                        if idx >= 4:
                            s.wait_ge(S["v_kn"].sem, bs["v_kn"] + idx - 3)
                        S["s_ke"].inc(s.activation(
                            ksb[:, g * 512:(g + 1) * 512],
                            pk[:, j * 512:(j + 1) * 512], AF.Identity,
                            bias=bkf[:, g:g + 1]))
                    s.wait_ge(S["v_kg"].sem, bs["v_kg"] + gi + 1)
                    if gi > 0:
                        s.wait_ge(S["t_kb"].sem, bs["t_kb"] + 4 * gi)
                    s.activation(fk[:], kssf[:], AF.Ln)
                    s.activation(fk[:], fk[:], AF.Exp, scale=0.5)
                    s.activation(fk[:], fk[:], AF.Ln, bias=1e-6)
                    S["s_kr"].inc(s.activation(fk[:], fk[:], AF.Exp,
                                               scale=-1.0))

            @blk.tensor
            def _(t, bs=bs):
                for gi in range(8):
                    for k in range(4):
                        sb = 4 * gi + k
                        j = sb % 2
                        t.wait_ge(S["s_ap"].sem, bs["s_ap"] + sb + 1)
                        if sb >= 1:
                            t.wait_ge(S["v_ef"].sem, bs["v_ef"] + sb)
                        for cc in range(16):
                            inst = t.transpose(
                                pt_ln[:, cc * 128:(cc + 1) * 128],
                                lntm[:, cc * 128:(cc + 1) * 128],
                                idn[:])
                        S["t_tr"].inc(inst)
                    for k in range(4):
                        sb = 4 * gi + k
                        j = sb % 2
                        t.wait_ge(S["v_ef"].sem, bs["v_ef"] + sb + 1)
                        if sb >= 2:
                            t.wait_ge(S["v_ve"].sem, bs["v_ve"] + sb - 1)
                        for cc in range(16):
                            inst = t.matmul(
                                pv[:, j * 512:(j + 1) * 512],
                                lnfm4[:, cc * 512 + k * 128:
                                      cc * 512 + (k + 1) * 128],
                                wvb[:, cc * 512:(cc + 1) * 512],
                                start=(cc == 0), stop=(cc == 15))
                        S["t_vm"].inc(inst)
                    t.wait_ge(S["v_ef"].sem, bs["v_ef"] + 4 * gi + 4)
                    for g in range(4):
                        idx = 4 * gi + g
                        j = idx % 2
                        if idx >= 2:
                            t.wait_ge(S["s_ke"].sem, bs["s_ke"] + idx - 1)
                        for cc in range(16):
                            inst = t.matmul(
                                pk[:, j * 512:(j + 1) * 512],
                                wkb[:, cc * 512 + g * 128:
                                    cc * 512 + (g + 1) * 128],
                                lnfm4[:, cc * 512:(cc + 1) * 512],
                                start=(cc == 0), stop=(cc == 15))
                        S["t_km"].inc(inst)
                    for g in range(4):
                        idx = 4 * gi + g
                        j = idx % 2
                        t.wait_ge(S["v_kq"].sem, bs["v_kq"] + idx + 1)
                        if g == 0 and gi > 0:
                            t.wait_ge(S["v_kg"].sem, bs["v_kg"] + gi)
                        S["t_ks"].inc(t.matmul(
                            pkss[:], hotk[:, 4 * g:4 * g + 4],
                            ksq[:, j * 512:(j + 1) * 512],
                            start=(g == 0), stop=(g == 3)))
                    t.wait_ge(S["s_kr"].sem, bs["s_kr"] + gi + 1)
                    for g in range(4):
                        idx = 4 * gi + g
                        if idx >= 1:
                            t.wait_ge(S["v_kn"].sem, bs["v_kn"] + idx)
                        S["t_kb"].inc(t.matmul(
                            pbc1[:], rs4[:, g * 128:(g + 1) * 128], fk[:],
                            start=True, stop=True))
        p1.close()
        qr = EA(nc.sbuf_tensor("qr", [128, H * 1024], BF16))

        if "kfm" in tap_aps:
            with nc.Block() as blk:
                @blk.sync
                def _(sync):
                    sem_o.dma(sync.dma_start(out=tap_aps["kfm"], in_=kfm[:]))
                    sem_o.dma(sync.dma_start(out=tap_aps["vtm"], in_=vtm[:]))
                    sync.wait_ge(sem_o.sem, sem_o.v)

        scan_chunks = []
        for rb in range(16):
            for cb in range(4):
                scan_chunks.append((0, A(wupt, rb * 128 * FI + cb * 1024,
                                         [[FI, 128], [1, 1024]])))
        for rb in range(32):
            for cb in range(2):
                scan_chunks.append((1, A(wdnt, rb * 128 * C + cb * 1024,
                                         [[C, 128], [1, 1024]])))
        scan_pos = [0]
        x2_tok = [0] * NT
        y_tok = {}
        h_tok = [0] * 4
        wdn_tok = [0, 0]

        for hf in range(2):
            p2 = contextlib.ExitStack()
            E2 = p2.enter_context
            lnh = E2(nc.sbuf_tensor(f"lnh{hf}", [128, 16 * 1024], BF16))
            hot = E2(nc.sbuf_tensor(f"hot{hf}", [128, 256], BF16))
            rs16 = E2(nc.sbuf_tensor(f"rs16{hf}", [16, 2048], F32))
            qct = E2(nc.sbuf_tensor(f"qct{hf}", [128, 1024], F32))
            qst = E2(nc.sbuf_tensor(f"qst{hf}", [128, 1024], F32))
            xt2 = E2(nc.sbuf_tensor(f"xt2{hf}", [128, 2048], F32))
            lnt2b = E2(nc.sbuf_tensor(f"lnt2b{hf}", [128, 2048], BF16))
            dsc2 = E2(nc.sbuf_tensor(f"dsc2{hf}", [128, 2048], BF16))
            lnt2 = E2(nc.sbuf_tensor(f"lnt2{hf}", [128, 2 * 6], F32))
            wst2 = E2(nc.sbuf_tensor(f"wst2{hf}", [128, 2 * 2048], F32))
            wqb = E2(nc.sbuf_tensor(f"wqb{hf}", [128, 2 * 2048], BF16))
            qbf = E2(nc.sbuf_tensor(f"qbf{hf}", [128, 2 * 1024], BF16))
            qsqb = E2(nc.sbuf_tensor(f"qsqb{hf}", [128, 2 * 1024], BF16))
            fq = E2(nc.sbuf_tensor(f"fq{hf}", [16, 1024], F32))
            qswb = E2(nc.sbuf_tensor(f"qswb{hf}", [128, 1024], BF16))
            qrt = E2(nc.sbuf_tensor(f"qrt{hf}", [128, 1024], BF16))
            pre_ctx = contextlib.ExitStack()
            pt_ln2 = pre_ctx.enter_context(
                nc.psum_tensor(f"pt_ln2{hf}", [128, 2 * 2048], BF16))

            bs = snap()
            bd = sem_d.v
            with nc.Block() as blk:
                @blk.sync
                def _(sync, hf=hf, bs=bs):
                    sem_d.dma(sync.dma_start(
                        out=qct[:], in_=A(qcos, hf * 1024, [[TOK, 128],
                                                            [1, 1024]])))
                    sem_d.dma(sync.dma_start(
                        out=qst[:], in_=A(qsin, hf * 1024, [[TOK, 128],
                                                            [1, 1024]])))
                    sem_d.dma(sync.dma_start(out=rs16[:], in_=rsel16))
                    sem_d.dma(sync.dma_start(out=hot[:], in_=hot16))
                    for k in range(8):
                        ti = hf * 8 + k
                        if k >= 1:
                            sync.wait_ge(S["s_ap"].sem, bs["s_ap"] + k)
                        sem_d.dma(sync.dma_start(
                            out=xt2[:],
                            in_=x_own[ti * 128:(ti + 1) * 128, :]))

                @blk.vector
                def _(v, bs=bs, bd=bd):
                    for k in range(8):
                        ln_vec(v, xt2[:], dsc2, lnt2, 0, k, bs,
                               bd + 16 * (k + 5), nbuf=1)
                        v.wait_ge(S["t_tr"].sem, bs["t_tr"] + k + 1)
                        S["v_ef"].inc(v.tensor_copy(
                            A(lnh, k * 128, [[16 * 1024, 128], [1024, 16],
                                             [1, 128]]),
                            A(pt_ln2, (k % 2) * 2048,
                              [[2 * 2048, 128], [128, 16], [1, 128]])))

                @blk.scalar
                def _(s, bs=bs):
                    for k in range(8):
                        if k >= 1:
                            s.wait_ge(S["t_tr"].sem, bs["t_tr"] + k)
                        ln_scalar(s, xt2[:], lnt2b[:], lnt2, 0, k, bs)

                @blk.tensor
                def _(t, bs=bs):
                    for k in range(8):
                        j = k % 2
                        t.wait_ge(S["s_ap"].sem, bs["s_ap"] + k + 1)
                        if k >= 2:
                            t.wait_ge(S["v_ef"].sem, bs["v_ef"] + k - 1)
                        for cc in range(16):
                            inst = t.transpose(
                                pt_ln2[:, j * 2048 + cc * 128:
                                       j * 2048 + (cc + 1) * 128],
                                lnt2b[:, cc * 128:(cc + 1) * 128],
                                idn[:])
                        S["t_tr"].inc(inst)

            pre_ctx.close()
            pq = E2(nc.psum_tensor(f"pq{hf}", [128, 2 * 1024], F32))
            pss = E2(nc.psum_tensor(f"pss{hf}", [16, 1024], F32))
            pbc2 = E2(nc.psum_tensor(f"pbc2{hf}", [128, 1024], F32))

            bs = snap()
            bd = sem_d.v
            with nc.Block() as blk:
                @blk.sync
                def _(sync, bs=bs):
                    for h in range(H):
                        if h >= 2:
                            sync.wait_ge(S["v_wc"].sem, bs["v_wc"] + h - 1)
                        src = A(wq, h * 128, [[C, 128], [128 * C, 16],
                                              [1, 128]])
                        sem_d.dma(sync.dma_start(
                            out=wst2[:, (h % 2) * 2048:(h % 2 + 1) * 2048],
                            in_=src))

                @blk.vector
                def _(v, hf=hf, bs=bs, bd=bd):
                    for h in range(H):
                        j = h % 2
                        v.wait_ge(sem_d.sem, bd + 16 * (h + 1))
                        if h >= 2:
                            v.wait_ge(S["t_qm"].sem, bs["t_qm"] + h - 1)
                        S["v_wc"].inc(v.tensor_copy(
                            wqb[:, j * 2048:(j + 1) * 2048],
                            wst2[:, j * 2048:(j + 1) * 2048]))
                        v.wait_ge(S["s_qe"].sem, bs["s_qe"] + h + 1)
                        S["v_qq"].inc(v.tensor_mul(
                            qsqb[:, j * 1024:(j + 1) * 1024],
                            qbf[:, j * 1024:(j + 1) * 1024],
                            qbf[:, j * 1024:(j + 1) * 1024]))
                        v.tensor_copy(qswb[0:64, :],
                                      qbf[64:128, j * 1024:(j + 1) * 1024])
                        v.tensor_copy(qswb[64:128, :],
                                      qbf[0:64, j * 1024:(j + 1) * 1024])
                        v.tensor_mul(qrt[:], qbf[:, j * 1024:(j + 1) * 1024],
                                     qct[:])
                        v.tensor_mul(qswb[:], qswb[:], qst[:])
                        S["v_rp"].inc(v.tensor_add(
                            qr[:, h * 1024:(h + 1) * 1024],
                            qrt[:], qswb[:]))
                    v.wait_ge(S["t_ss"].sem, bs["t_ss"] + H)
                    S["v_sg"].inc(v.tensor_scalar(fq[:], pss[:], 1e-30, None,
                                                  ALU.max))
                    v.wait_ge(S["s_rm"].sem, bs["s_rm"] + 1)
                    S["v_s2"].inc(v.tensor_scalar(fq[:], fq[:], al16[:, 0:1],
                                                  None, ALU.mult))
                    for h in range(H):
                        v.wait_ge(S["t_b2"].sem, bs["t_b2"] + h + 1)
                        S["v_s2"].inc(v.tensor_mul(
                            qr[:, h * 1024:(h + 1) * 1024],
                            qr[:, h * 1024:(h + 1) * 1024], pbc2[:]))

                @blk.scalar
                def _(s, bs=bs):
                    for h in range(H):
                        j = h % 2
                        s.wait_ge(S["t_qm"].sem, bs["t_qm"] + h + 1)
                        if h >= 2:
                            s.wait_ge(S["v_rp"].sem, bs["v_rp"] + h - 1)
                        S["s_qe"].inc(s.activation(
                            qbf[:, j * 1024:(j + 1) * 1024],
                            pq[:, j * 1024:(j + 1) * 1024], AF.Identity,
                            bias=bqf[:, h:h + 1]))
                    s.wait_ge(S["v_sg"].sem, bs["v_sg"] + 1)
                    s.activation(fq[:], fq[:], AF.Ln)
                    s.activation(fq[:], fq[:], AF.Exp, scale=0.5)
                    s.activation(fq[:], fq[:], AF.Ln, bias=1e-6)
                    S["s_rm"].inc(s.activation(fq[:], fq[:], AF.Exp,
                                               scale=-1.0))

                @blk.tensor
                def _(t, bs=bs):
                    for h in range(H):
                        j = h % 2
                        t.wait_ge(S["v_wc"].sem, bs["v_wc"] + h + 1)
                        if h >= 2:
                            t.wait_ge(S["s_qe"].sem, bs["s_qe"] + h - 1)
                        for cc in range(16):
                            for n in range(2):
                                inst = t.matmul(
                                    pq[:, j * 1024 + n * 512:
                                       j * 1024 + (n + 1) * 512],
                                    wqb[:, j * 2048 + cc * 128:
                                        j * 2048 + (cc + 1) * 128],
                                    lnh[:, cc * 1024 + n * 512:
                                        cc * 1024 + (n + 1) * 512],
                                    start=(cc == 0), stop=(cc == 15))
                        S["t_qm"].inc(inst)
                        t.wait_ge(S["v_qq"].sem, bs["v_qq"] + h + 1)
                        for n in range(2):
                            inst = t.matmul(
                                pss[:, n * 512:(n + 1) * 512],
                                hot[:, h * 16:(h + 1) * 16],
                                qsqb[:, j * 1024 + n * 512:
                                     j * 1024 + (n + 1) * 512],
                                start=(h == 0), stop=(h == H - 1))
                        S["t_ss"].inc(inst)
                    for h in range(H):
                        t.wait_ge(S["s_rm"].sem, bs["s_rm"] + 1)
                        t.wait_ge(S["v_s2"].sem, bs["v_s2"] + h + 1)
                        for n in range(2):
                            inst = t.matmul(
                                pbc2[:, n * 512:(n + 1) * 512],
                                rs16[:, h * 128:(h + 1) * 128],
                                fq[:, n * 512:(n + 1) * 512],
                                start=True, stop=True)
                        S["t_b2"].inc(inst)

            p2.close()

            if hf == 0 and "qra" in tap_aps:
                with nc.Block() as blk:
                    @blk.sync
                    def _(sync):
                        sem_o.dma(sync.dma_start(out=tap_aps["qra"],
                                                 in_=qr[:]))
                        if "fqa" in tap_aps:
                            sem_o.dma(sync.dma_start(out=tap_aps["fqa"],
                                                     in_=fq[:]))
                            sem_o.dma(sync.dma_start(out=tap_aps["qcta"],
                                                     in_=qct[:]))
                            sem_o.dma(sync.dma_start(out=tap_aps["qsta"],
                                                     in_=qst[:]))
                        sync.wait_ge(sem_o.sem, sem_o.v)

            p3a = contextlib.ExitStack()
            E3 = p3a.enter_context
            em = E3(nc.sbuf_tensor(f"em{hf}", [128, 8 * 512], BF16))
            ptb = E3(nc.sbuf_tensor(f"ptb{hf}", [128, 2 * 512], BF16))
            ystg = E3(nc.sbuf_tensor(f"ystg{hf}", [128, 2 * 512], BF16))
            ytmp = E3(nc.sbuf_tensor(f"ytmp{hf}", [128, 2 * 512], BF16))
            scs = E3(nc.sbuf_tensor(f"scs{hf}", [128, 2 * 1024], F32))
            rden0 = E3(nc.sbuf_tensor(f"rden0{hf}", [1, 512], F32))
            rden1 = E3(nc.sbuf_tensor(f"rden1{hf}", [1, 512], F32))
            rdt = E3(nc.sbuf_tensor(f"rdt{hf}", [1, 512], F32))
            ps = E3(nc.psum_tensor(f"ps{hf}", [128, 2 * 512], F32))
            py = E3(nc.psum_tensor(f"py{hf}", [128, 2 * 512], F32))
            pb = E3(nc.psum_tensor(f"pb{hf}", [128, 2 * 512], F32))
            pd0 = E3(nc.psum_tensor(f"pd0{hf}", [1, 512], F32))
            pd1 = E3(nc.psum_tensor(f"pd1{hf}", [1, 512], F32))

            with nc.Block() as blk:
                @blk.sync
                def _(sync):
                    sem_d.dma(sync.dma_start(out=em[:], in_=emasks))
                    sync.wait_ge(sem_d.sem, sem_d.v)

            for bj in range(2):
                bi = hf * 2 + bj
                NB = 8 * bi + 8
                qc0 = 512 * bj
                bs = snap()
                bd = sem_d.v
                nsc = min(32, len(scan_chunks) - scan_pos[0])

                def _avden(t, h, b, bs, NB, bi, g):
                    ii = h * NB + b
                    if b >= 8 * bi:
                        e = b - 8 * bi
                        t.wait_ge(S["v_mk"].sem, bs["v_mk"] + h * 8 + e + 1)
                    else:
                        t.wait_ge(S["s_ex"].sem, bs["s_ex"] + ii + 1)
                    if b == 0 and h >= 2:
                        t.wait_ge(S["v_ye"].sem, bs["v_ye"] + h - 1)
                    S["t_av"].inc(t.matmul(
                        py[:, (h % 2) * 512:(h % 2 + 1) * 512],
                        vtm[:, g * T + b * 128:g * T + (b + 1) * 128],
                        ptb[:, (b % 2) * 512:(b % 2 + 1) * 512],
                        start=(b == 0), stop=(b == NB - 1)))
                    if b == 0 and h >= 2:
                        t.wait_ge(S["s_dr"].sem, bs["s_dr"] + h - 1)
                    S["t_dn"].inc(t.matmul(
                        (pd0 if h % 2 == 0 else pd1)[:], onescb[:],
                        ptb[:, (b % 2) * 512:(b % 2 + 1) * 512],
                        start=(b == 0), stop=(b == NB - 1)))

                with nc.Block() as blk:
                    @blk.sync
                    def _(sync, bs=bs, nsc=nsc):
                        for ch in range(nsc):
                            pos = scan_pos[0] + ch
                            if ch >= 2:
                                sync.wait_ge(S["v_sc"].sem,
                                             bs["v_sc"] + ch - 1)
                            sem_d.dma(sync.dma_start(
                                out=scs[:, (ch % 2) * 1024:
                                        (ch % 2 + 1) * 1024],
                                in_=scan_chunks[pos][1]))

                    @blk.tensor
                    def _(t, bi=bi, NB=NB, qc0=qc0, bs=bs):
                        for h in range(H):
                            g = h // 4
                            for b in range(NB):
                                ii = h * NB + b
                                if ii >= 2:
                                    t.wait_ge(S["s_ex"].sem,
                                              bs["s_ex"] + ii - 1)
                                S["t_sc"].inc(t.matmul(
                                    ps[:, (b % 2) * 512:(b % 2 + 1) * 512],
                                    kfm[:, g * T + b * 128:
                                        g * T + (b + 1) * 128],
                                    qr[:, h * 1024 + qc0:
                                       h * 1024 + qc0 + 512],
                                    start=True, stop=True))
                                if b >= 1:
                                    _avden(t, h, b - 1, bs, NB, bi, g)
                            _avden(t, h, NB - 1, bs, NB, bi, g)
                            t.wait_ge(S["s_dr"].sem, bs["s_dr"] + h + 1)
                            if h >= 2:
                                t.wait_ge(S["v_ye"].sem, bs["v_ye"] + h - 1)
                            S["t_b"].inc(t.matmul(
                                pb[:, (h % 2) * 512:(h % 2 + 1) * 512],
                                ones1f[:],
                                (rden0 if h % 2 == 0 else rden1)[0:1, :],
                                start=True, stop=True))

                    @blk.scalar
                    def _(s, NB=NB, bs=bs):
                        for h in range(H):
                            for b in range(NB):
                                ii = h * NB + b
                                s.wait_ge(S["t_sc"].sem, bs["t_sc"] + ii + 1)
                                if ii >= 2:
                                    s.wait_ge(S["t_av"].sem,
                                              bs["t_av"] + ii - 1)
                                S["s_ex"].inc(s.activation(
                                    ptb[:, (b % 2) * 512:(b % 2 + 1) * 512],
                                    ps[:, (b % 2) * 512:(b % 2 + 1) * 512],
                                    AF.Exp, bias=negc[:, 0:1]))
                            s.wait_ge(S["t_dn"].sem,
                                      bs["t_dn"] + (h + 1) * NB)
                            s.activation(rdt[:],
                                         (pd0 if h % 2 == 0 else pd1)[0:1, :],
                                         AF.Ln)
                            S["s_dr"].inc(s.activation(
                                (rden0 if h % 2 == 0 else rden1)[:], rdt[:],
                                AF.Exp, scale=-1.0))
                            if h >= 2:
                                s.wait_ge(S["v_ye"].sem, bs["v_ye"] + h - 1)
                                sem_sp.dma(s.dma_start(
                                    out=A(y_scr, (h - 2) * TOK + bi * 512,
                                          [[H * TOK, 128], [1, 512]]),
                                    in_=ystg[:, (h % 2) * 512:
                                             (h % 2 + 1) * 512]))
                                y_tok[(h - 2, bi)] = sem_sp.v
                        for h in (H - 2, H - 1):
                            s.wait_ge(S["v_ye"].sem, bs["v_ye"] + h + 1)
                            sem_sp.dma(s.dma_start(
                                out=A(y_scr, h * TOK + bi * 512,
                                      [[H * TOK, 128], [1, 512]]),
                                in_=ystg[:, (h % 2) * 512:(h % 2 + 1) * 512]))
                            y_tok[(h, bi)] = sem_sp.v

                    @blk.vector
                    def _(v, bi=bi, NB=NB, bs=bs, bd=bd, nsc=nsc):
                        for ch in range(nsc):
                            pos = scan_pos[0] + ch
                            v.wait_ge(sem_d.sem, bd + 16 * (ch + 1))
                            S["v_sc"].inc(v.tensor_reduce(
                                rts[:, pos:pos + 1],
                                scs[:, (ch % 2) * 1024:(ch % 2 + 1) * 1024],
                                AX.X, ALU.add, apply_absolute_value=True))
                        for h in range(H):
                            for e in range(8):
                                b = 8 * bi + e
                                ii = h * NB + b
                                v.wait_ge(S["s_ex"].sem, bs["s_ex"] + ii + 1)
                                S["v_mk"].inc(v.tensor_mul(
                                    ptb[:, (b % 2) * 512:(b % 2 + 1) * 512],
                                    ptb[:, (b % 2) * 512:(b % 2 + 1) * 512],
                                    em[:, e * 512:(e + 1) * 512]))
                            v.wait_ge(S["t_b"].sem, bs["t_b"] + h + 1)
                            v.wait_ge(S["t_av"].sem, bs["t_av"] + (h + 1) * NB)
                            if h >= 2:
                                v.wait_ge(sem_sp.sem, y_tok[(h - 2, bi)])
                            v.tensor_copy(
                                ytmp[:, (h % 2) * 512:(h % 2 + 1) * 512],
                                py[:, (h % 2) * 512:(h % 2 + 1) * 512])
                            S["v_ye"].inc(v.tensor_mul(
                                ystg[:, (h % 2) * 512:(h % 2 + 1) * 512],
                                ytmp[:, (h % 2) * 512:(h % 2 + 1) * 512],
                                pb[:, (h % 2) * 512:(h % 2 + 1) * 512]))
                scan_pos[0] += nsc
            p3a.close()

        attn_ctx.close()

        a1r_ctx = contextlib.ExitStack()
        a1r = a1r_ctx.enter_context(nc.sbuf_tensor("a1r", [1, 2048], F32))
        yb_ctx = contextlib.ExitStack()
        yb = yb_ctx.enter_context(nc.sbuf_tensor("yb", [128, H * TOK], BF16))
        p3b = contextlib.ExitStack()
        E4 = p3b.enter_context
        bor = E4(nc.sbuf_tensor("bor", [1, C], F32))
        wob = E4(nc.sbuf_tensor("wob", [128, 16 * 2048], BF16))
        bob = E4(nc.sbuf_tensor("bob", [128, 2048], F32))
        xt3 = E4(nc.sbuf_tensor("xt3", [128, 2 * 2048], F32))
        with nc.Block() as blk:
            bs = snap()
            bd = sem_d.v

            @blk.sync
            def _(sync, bs=bs):
                sync.wait_ge(sem_sp.sem, sem_sp.v)
                sem_d.dma(sync.dma_start(out=bor[0:1, :], in_=bo))
                sem_d.dma(sync.dma_start(out=yb[:], in_=y_scr))
                for hc in range(16):
                    if hc >= 2:
                        sync.wait_ge(S["v_wc"].sem, bs["v_wc"] + hc - 1)
                    sem_d.dma(sync.dma_start(
                        out=xt3[:, (hc % 2) * 2048:(hc % 2 + 1) * 2048],
                        in_=wo[hc * 128:(hc + 1) * 128, :]))

            @blk.vector
            def _(v, bs=bs, bd=bd):
                for hc in range(16):
                    v.wait_ge(sem_d.sem, bd + 16 * (hc + 3))
                    S["v_wc"].inc(v.tensor_copy(
                        wob[:, hc * 2048:(hc + 1) * 2048],
                        xt3[:, (hc % 2) * 2048:(hc % 2 + 1) * 2048]))

        if "yb" in tap_aps:
            with nc.Block() as blk:
                @blk.sync
                def _(sync):
                    sem_o.dma(sync.dma_start(out=tap_aps["yb"], in_=yb[:]))
                    sync.wait_ge(sem_o.sem, sem_o.v)

        pboc = contextlib.ExitStack()
        pbo = pboc.enter_context(nc.psum_tensor("pbo", [128, 2048], F32))
        with nc.Block() as blk:
            @blk.tensor
            def _(t):
                for n in range(4):
                    t.matmul(pbo[:, n * 512:(n + 1) * 512], ones1f[:],
                             bor[0:1, n * 512:(n + 1) * 512],
                             start=True, stop=True)
        with nc.Block() as blk:
            @blk.vector
            def _(v):
                v.tensor_copy(bob[:], pbo[:])
        pboc.close()

        x2t = E4(nc.sbuf_tensor("x2t", [128, 2048], F32))
        ln2t = E4(nc.sbuf_tensor("ln2t", [128, 2048], BF16))
        ln2f = E4(nc.sbuf_tensor("ln2f", [128, 2048], BF16))
        w1 = E4(nc.sbuf_tensor("w1", [128, 2048], BF16))
        bb0 = E4(nc.sbuf_tensor("bb0", [128, 2048], BF16))
        bb1 = E4(nc.sbuf_tensor("bb1", [128, 2048], BF16))
        mt = E4(nc.sbuf_tensor("mt", [128, 128], F32))
        sh3 = E4(nc.sbuf_tensor("sh3", [64, 128], F32))
        scl = E4(nc.sbuf_tensor("scl", [128, 128], F32))

        lnt3 = E4(nc.sbuf_tensor("lnt3", [128, 2 * 6], F32))
        rub = E4(nc.sbuf_tensor("rub", [128, 256], F32))
        rv32 = E4(nc.sbuf_tensor("rv32", [128, 256], U32))
        rtq = E4(nc.sbuf_tensor("rtq", [128, 256], F32))
        rgq = E4(nc.sbuf_tensor("rgq", [128, 256], BF16))
        rt2 = E4(nc.sbuf_tensor("rt2", [128, 256], BF16))
        po = E4(nc.psum_tensor("po", [128, 2048], F32))
        ptl3 = E4(nc.psum_tensor("ptl3", [128, 2048], BF16))

        bs = snap()
        bd = sem_d.v
        spv0 = sem_sp.v
        with nc.Block() as blk:
            @blk.sync
            def _(sync, bs=bs):
                for i in range(NT):
                    if i >= 2:
                        sync.wait_ge(S["v_x2"].sem, bs["v_x2"] + i - 1)
                    sem_d.dma(sync.dma_start(
                        out=xt3[:, (i % 2) * 2048:(i % 2 + 1) * 2048],
                        in_=x_own[i * 128:(i + 1) * 128, :]))

            @blk.tensor
            def _(t, bs=bs):
                for i in range(NT):
                    if i >= 1:
                        t.wait_ge(S["v_w1"].sem, bs["v_w1"] + i)
                    for hc in range(16):
                        for n in range(4):
                            inst = t.matmul(
                                po[:, n * 512:(n + 1) * 512],
                                A(yb, hc * 2048 + i * 128,
                                  [[H * TOK, 128], [1, 128]]),
                                wob[:, hc * 2048 + n * 512:
                                    hc * 2048 + (n + 1) * 512],
                                start=(hc == 0), stop=(hc == 15))
                    S["t_op"].inc(inst)
                    t.wait_ge(S["s_l2"].sem, bs["s_l2"] + i + 1)
                    if i >= 1:
                        t.wait_ge(S["v_lf"].sem, bs["v_lf"] + i)
                    for cc in range(16):
                        inst = t.transpose(
                            ptl3[:, cc * 128:(cc + 1) * 128],
                            ln2t[:, cc * 128:(cc + 1) * 128], idn[:])
                    S["t_t2"].inc(inst)
                    t.wait_ge(S["v_lf"].sem, bs["v_lf"] + i + 1)
                    t.wait_ge(S["v_x2"].sem, bs["v_x2"] + i + 1)
                    for cc in range(16):
                        inst = t.matmul(
                            po[:, cc * 128:(cc + 1) * 128], hadm[:],
                            ln2f[:, cc * 128:(cc + 1) * 128],
                            start=True, stop=True)
                    S["t_wh"].inc(inst)


            @blk.scalar
            def _(s, bs=bs):
                for i in range(NT):
                    s.wait_ge(S["v_x2"].sem, bs["v_x2"] + i + 1)
                    sem_sp.dma(s.dma_start(
                        out=x2_scr[i * 128:(i + 1) * 128, :], in_=x2t[:]))
                    x2_tok[i] = sem_sp.v
                    if i >= 1:
                        s.wait_ge(S["t_t2"].sem, bs["t_t2"] + i)
                    ln_scalar(s, x2t[:], ln2t[:], lnt3, i % 2, i, bs,
                              nm_vst="v_st", nm_srs="s_rs", nm_vnm="v_nm",
                              nm_sap="s_l2")
                    s.wait_ge(S["v_a1"].sem, bs["v_a1"] + i + 1)
                    s.activation(scl[:], mt[:], AF.Ln)
                    S["s_sc"].inc(s.activation(
                        scl[:], scl[:], AF.Exp, scale=-1.0, bias=LN127))

            @blk.vector
            def _(v, bs=bs, bd=bd):
                for i in range(NT):
                    v.wait_ge(S["t_op"].sem, bs["t_op"] + i + 1)
                    v.wait_ge(sem_d.sem, bd + 16 * (i + 1))
                    if i >= 1:
                        v.wait_ge(sem_sp.sem, x2_tok[i - 1])
                        v.wait_ge(S["s_l2"].sem, bs["s_l2"] + i)
                    v.tensor_add(x2t[:], po[:], bob[:])
                    S["v_x2"].inc(v.tensor_add(
                        x2t[:], x2t[:], xt3[:, (i % 2) * 2048:
                                            (i % 2 + 1) * 2048]))
                    ln_vec(v, x2t[:], w1, lnt3, i % 2, i, bs,
                           bd, nm_vst="v_st", nm_srs="s_rs", nm_vnm="v_nm")
                    v.wait_ge(S["t_t2"].sem, bs["t_t2"] + i + 1)
                    S["v_lf"].inc(v.tensor_copy(ln2f[:], ptl3[:]))
                    v.wait_ge(S["t_wh"].sem, bs["t_wh"] + i + 1)
                    S["v_w1"].inc(v.tensor_copy(w1[:], po[:]))
                    v.tensor_add(bb0[:, 0:1024], w1[:, 0:1024], w1[:, 1024:2048])
                    v.tensor_sub(bb0[:, 1024:2048], w1[:, 0:1024],
                                 w1[:, 1024:2048])
                    for st in range(1, 4):
                        half = 16 >> (st + 1)
                        grp = 1 << st
                        src = bb0 if st % 2 == 1 else bb1
                        dst = bb1 if st % 2 == 1 else bb0
                        blkw = 16 // grp
                        dims = [[2048, 128], [blkw * 128, grp],
                                [1, half * 128]]
                        v.tensor_add(A(dst, 0, dims), A(src, 0, dims),
                                     A(src, half * 128, dims))
                        v.tensor_sub(A(dst, half * 128, dims), A(src, 0, dims),
                                     A(src, half * 128, dims))
                    xh = bb1
                    if i >= 1:
                        v.wait_ge(S["s_sc"].sem, bs["s_sc"] + i)
                    v.tensor_reduce(
                        mt[:], A(xh, 0, [[2048, 128], [1, 128], [128, 16]]),
                        AX.X, ALU.max, apply_absolute_value=True)
                    v.tensor_copy(sh3[0:64, :], mt[64:128, :])
                    v.tensor_tensor(mt[0:64, :], mt[0:64, :], sh3[0:64, :], ALU.max)
                    v.tensor_copy(sh3[0:32, :], mt[32:64, :])
                    v.tensor_tensor(mt[0:32, :], mt[0:32, :], sh3[0:32, :], ALU.max)
                    for stp in (16, 8, 4, 2, 1):
                        v.stream_shuffle(sh3[0:32, :], mt[0:32, :],
                                         [(i2 + stp) % 32 for i2 in range(32)])
                        v.tensor_tensor(mt[0:32, :], mt[0:32, :], sh3[0:32, :], ALU.max)
                    v.tensor_copy(mt[32:64, :], mt[0:32, :])
                    v.tensor_copy(mt[64:96, :], mt[0:32, :])
                    v.tensor_copy(mt[96:128, :], mt[0:32, :])
                    v.tensor_scalar(mt[:], mt[:], A1CLIP, None, ALU.max)
                    S["v_a1"].inc(v.tensor_copy(
                        a1r[0:1, i * 128:(i + 1) * 128], mt[0:1, :]))
                    v.wait_ge(S["s_sc"].sem, bs["s_sc"] + i + 1)
                    v.tensor_mul(rub[:, 0:128], mt[:], scl[:])
                    v.tensor_scalar(rub[:, 0:128], rub[:, 0:128],
                                    -1.0 / 127.0, 2.0, ALU.mult, ALU.add)
                    v.tensor_mul(scl[:], scl[:], rub[:, 0:128])
                    for sub in range(8):
                        xs = A(xh, sub * 2 * 128, [[2048, 128], [128, 2],
                                                   [1, 128]])
                        v.tensor_tensor(rub[:], xs,
                                        A(scl, 0, [[128, 128], [0, 2],
                                                   [1, 128]]), ALU.mult)
                        v.tensor_scalar(rv32[:].bitcast(F32), rub[:], 384.0,
                                        None, ALU.add)
                        v.tensor_scalar(rv32[:], rv32[:], FMASK, None,
                                        ALU.bitwise_and)
                        v.tensor_scalar(rtq[:], rv32[:].bitcast(F32), -383.5,
                                        None, ALU.add)
                        v.tensor_tensor(rgq[:], rub[:], rtq[:], ALU.is_ge)
                        inst = v.scalar_tensor_tensor(
                            A(yb, sub * 2 * 2048 + i * 128,
                              [[H * TOK, 128], [2048, 2], [1, 128]]),
                            rtq[:], -0.5, rgq[:], ALU.add, ALU.add)
                    S["v_q1"].inc(inst)
        if "ln2f" in tap_aps:
            with nc.Block() as blk:
                @blk.sync
                def _(sync):
                    sem_o.dma(sync.dma_start(out=tap_aps["ln2f"], in_=ln2f[:]))
                    sem_o.dma(sync.dma_start(out=tap_aps["w1"], in_=w1[:]))
                    sem_o.dma(sync.dma_start(out=tap_aps["xh"], in_=bb1[:]))
                    sem_o.dma(sync.dma_start(out=tap_aps["mtx"], in_=mt[:]))
                    sync.wait_ge(sem_o.sem, sem_o.v)
        p3b.close()

        if "q1" in tap_aps:
            with nc.Block() as blk:
                @blk.sync
                def _(sync):
                    sem_o.dma(sync.dma_start(out=tap_aps["q1"], in_=yb[:]))
                    sync.wait_ge(sem_o.sem, sem_o.v)
        if "x2s" in tap_aps:
            with nc.Block() as blk:
                @blk.sync
                def _(sync):
                    sem_o.dma(sync.dma_start(out=tap_aps["x2s"], in_=x2_scr))
                    sem_o.dma(sync.dma_start(out=tap_aps["ybs"], in_=y_scr))
                    sync.wait_ge(sem_o.sem, sem_o.v)

        p6 = contextlib.ExitStack()
        E6 = p6.enter_context
        pal = E6(nc.psum_tensor("pal", [1, 128], F32))
        pbal = E6(nc.psum_tensor("pbal", [128, 8], F32))
        with nc.Block() as blk:
            @blk.tensor
            def _(t):
                t.matmul(pal[0:1, :], onescf[:], rts[:], start=True, stop=True)
        with nc.Block() as blk:
            @blk.vector
            def _(v):
                v.tensor_reduce(tsc[0:1, 9:10], pal[0:1, 0:64], AX.X, ALU.add)
                v.tensor_reduce(tsc[0:1, 10:11], pal[0:1, 64:128], AX.X,
                                ALU.add)
        with nc.Block() as blk:
            @blk.vector
            def _(v):
                v.tensor_scalar(tsc[0:1, 2:3], tsc[0:1, 9:10], 1.0 / WMEAN_N,
                                1e-5, ALU.mult, ALU.max)
                v.tensor_scalar(tsc[0:1, 3:4], tsc[0:1, 10:11], 1.0 / WMEAN_N,
                                1e-5, ALU.mult, ALU.max)
                v.tensor_scalar(tsc[0:1, 4:5], tsc[0:1, 9:10], 0.5 / WMEAN_N,
                                0.5e-5, ALU.mult, ALU.max)
                v.tensor_scalar(tsc[0:1, 6:7], tsc[0:1, 10:11], 0.5 / WMEAN_N,
                                0.5e-5, ALU.mult, ALU.max)
        with nc.Block() as blk:
            @blk.vector
            def _(v):
                v.tensor_scalar(tsc[0:1, 5:6], tsc[0:1, 4:5], -1.0, None,
                                ALU.mult)
                v.tensor_scalar(tsc[0:1, 7:8], tsc[0:1, 6:7], -1.0, None,
                                ALU.mult)
                v.tensor_mul(tsc[0:1, 11:12], tsc[0:1, 2:3], tsc[0:1, 2:3])
                v.tensor_scalar(tsc[0:1, 12:13], tsc[0:1, 3:4], GCONST, None,
                                ALU.mult)
        with nc.Block() as blk:
            @blk.vector
            def _(v):
                v.tensor_mul(tsc[0:1, 8:9], tsc[0:1, 11:12], tsc[0:1, 12:13])
        with nc.Block() as blk:
            @blk.tensor
            def _(t):
                for j in range(4):
                    t.matmul(pbal[:, j:j + 1], ones1f[:],
                             tsc[0:1, 4 + j:5 + j], start=True, stop=True)
        with nc.Block() as blk:
            @blk.vector
            def _(v):
                v.tensor_copy(thr[:, 0:4], pbal[:, 0:4])

        wst6 = E6(nc.sbuf_tensor("wst6", [128, 2 * 2048], F32))
        wt6 = E6(nc.sbuf_tensor("wt6", [128, 2 * 2048], BF16))
        tgp = E6(nc.sbuf_tensor("tgp", [128, 2048], F32))
        tgn = E6(nc.sbuf_tensor("tgn", [128, 2048], F32))
        hrl = E6(nc.sbuf_tensor("hrl", [128, 2 * 512], F32))
        hbf = E6(nc.sbuf_tensor("hbf", [128, 2 * 512], BF16))
        ph = E6(nc.psum_tensor("ph", [128, 2 * 512], F32))

        bs = snap()
        bd = sem_d.v
        h_sp = {}
        with nc.Block() as blk:
            @blk.sync
            def _(sync, bs=bs):
                for a in range(32):
                    if a >= 2:
                        sync.wait_ge(S["v_t6"].sem, bs["v_t6"] + a - 1)
                    sem_d.dma(sync.dma_start(
                        out=wst6[:, (a % 2) * 2048:(a % 2 + 1) * 2048],
                        in_=A(wupt, a * 128, [[FI, 128], [128 * FI, 16],
                                              [1, 128]])))

            @blk.scalar
            def _(s, bs=bs):
                for a in range(32):
                    for tb in range(4):
                        ii = 4 * a + tb
                        s.wait_ge(S["t_u6"].sem, bs["t_u6"] + ii + 1)
                        if ii >= 2:
                            s.wait_ge(S["v_h6"].sem, bs["v_h6"] + ii - 1)
                        S["s_r6"].inc(s.activation(
                            hrl[:, (tb % 2) * 512:(tb % 2 + 1) * 512],
                            ph[:, (tb % 2) * 512:(tb % 2 + 1) * 512],
                            AF.Relu))
                        s.wait_ge(S["v_h6"].sem, bs["v_h6"] + ii + 1)
                        inst = s.dma_start(
                            out=A(h_scr, a * 128 * TOK + tb * 512,
                                  [[TOK, 128], [1, 512]]),
                            in_=hbf[:, (tb % 2) * 512:(tb % 2 + 1) * 512])
                        sem_sp.dma(inst)
                        h_sp[ii] = sem_sp.v
                        h_tok[tb] = sem_sp.v

            @blk.vector
            def _(v, bs=bs, bd=bd):
                for a in range(32):
                    j = a % 2
                    v.wait_ge(sem_d.sem, bd + 16 * (a + 1))
                    if a >= 2:
                        v.wait_ge(S["t_u6"].sem, bs["t_u6"] + 4 * (a - 1))
                    ws = wst6[:, j * 2048:(j + 1) * 2048]
                    v.tensor_scalar(tgp[:], ws, thr[:, 0:1], None, ALU.is_ge)
                    v.tensor_scalar(tgn[:], ws, thr[:, 1:2], None, ALU.is_lt)
                    S["v_t6"].inc(v.tensor_sub(wt6[:, j * 2048:(j + 1) * 2048],
                                               tgp[:], tgn[:]))
                    for tb in range(4):
                        ii = 4 * a + tb
                        v.wait_ge(S["s_r6"].sem, bs["s_r6"] + ii + 1)
                        if ii >= 2:
                            v.wait_ge(sem_sp.sem, h_sp[ii - 2])
                        S["v_h6"].inc(v.tensor_mul(
                            hbf[:, (tb % 2) * 512:(tb % 2 + 1) * 512],
                            hrl[:, (tb % 2) * 512:(tb % 2 + 1) * 512],
                            hrl[:, (tb % 2) * 512:(tb % 2 + 1) * 512]))

            @blk.tensor
            def _(t, bs=bs):
                for a in range(32):
                    t.wait_ge(S["v_t6"].sem, bs["v_t6"] + a + 1)
                    for tb in range(4):
                        ii = 4 * a + tb
                        if ii >= 2:
                            t.wait_ge(S["s_r6"].sem, bs["s_r6"] + ii - 1)
                        for cc in range(16):
                            inst = t.matmul(
                                ph[:, (tb % 2) * 512:(tb % 2 + 1) * 512],
                                wt6[:, (a % 2) * 2048 + cc * 128:
                                    (a % 2) * 2048 + (cc + 1) * 128],
                                A(yb, cc * 2048 + tb * 512,
                                  [[H * TOK, 128], [1, 512]]),
                                start=(cc == 0), stop=(cc == 15))
                        S["t_u6"].inc(inst)
        if "tsc" in tap_aps:
            with nc.Block() as blk:
                @blk.sync
                def _(sync):
                    sem_o.dma(sync.dma_start(out=tap_aps["tsc"], in_=tsc[:]))
                    if "thr" in tap_aps:
                        sem_o.dma(sync.dma_start(out=tap_aps["thr"],
                                                 in_=thr[:]))
                    if "wt6" in tap_aps:
                        sem_o.dma(sync.dma_start(out=tap_aps["wt6"],
                                                 in_=wt6[:]))
                    sync.wait_ge(sem_o.sem, sem_o.v)
        p6.close()
        yb_ctx.close()

        p7 = contextlib.ExitStack()
        E7 = p7.enter_context
        a2r = E7(nc.sbuf_tensor("a2r", [1, 2048], F32))
        q2b = E7(nc.sbuf_tensor("q2b", [128, 32 * 1024], BF16))
        wdnb = E7(nc.sbuf_tensor("wdnb", [128, 32 * 1024], BF16))
        hb7 = E7(nc.sbuf_tensor("hb7", [128, 4096], BF16))
        zb7 = E7(nc.sbuf_tensor("zb7", [128, 4096], BF16))
        zc7 = E7(nc.sbuf_tensor("zc7", [128, 4096], BF16))
        m7 = E7(nc.sbuf_tensor("m7", [128, 128], F32))
        sh7 = E7(nc.sbuf_tensor("sh7", [64, 128], F32))
        scl7 = E7(nc.sbuf_tensor("scl7", [128, 128], F32))
        gcols = E7(nc.sbuf_tensor("gcols", [128, 8], F32))
        wst7 = E7(nc.sbuf_tensor("wst7", [128, 1024], F32))
        tg7p = E7(nc.sbuf_tensor("tg7p", [128, 512], F32))
        tg7n = E7(nc.sbuf_tensor("tg7n", [128, 512], F32))
        x2r = E7(nc.sbuf_tensor("x2r", [128, 2 * 1024], F32))
        ofn = E7(nc.sbuf_tensor("ofn", [128, 2 * 1024], F32))
        ru7 = E7(nc.sbuf_tensor("ru7", [128, 256], F32))
        rv7 = E7(nc.sbuf_tensor("rv7", [128, 256], U32))
        rt7 = E7(nc.sbuf_tensor("rt7", [128, 256], F32))
        rg7 = E7(nc.sbuf_tensor("rg7", [128, 256], BF16))
        rt27 = E7(nc.sbuf_tensor("rt27", [128, 256], BF16))
        pz = E7(nc.psum_tensor("pz", [128, 2 * 1024], F32))

        out_tok = [0] * NT

        for Hh in range(2):
            bs = snap()
            bd = sem_d.v
            with nc.Block() as blk:
                @blk.sync
                def _(sync, Hh=Hh, bs=bs):
                    for ti in range(8):
                        tl = Hh * 8 + ti
                        sync.wait_ge(sem_sp.sem, h_tok[tl // 4])
                        if ti >= 1:
                            sync.wait_ge(S["t_z7"].sem,
                                         bs["t_z7"] + 4 * ti)
                        sem_d.dma(sync.dma_start(
                            out=hb7[:],
                            in_=A(h_scr, tl * 128,
                                  [[TOK, 128], [128 * TOK, 32], [1, 128]])))

                @blk.tensor
                def _(t, bs=bs, bd=bd):
                    for ti in range(8):
                        for r in range(4):
                            t.wait_ge(sem_d.sem, bd + 16 * (ti + 1))
                            ri = 4 * ti + r
                            if ri >= 2:
                                t.wait_ge(S["v_z7"].sem, bs["v_z7"] + ri - 1)
                            for aa in range(8):
                                a = 8 * r + aa
                                inst = t.matmul(
                                    pz[:, (r % 2) * 1024 + aa * 128:
                                       (r % 2) * 1024 + (aa + 1) * 128],
                                    hadm[:],
                                    hb7[:, a * 128:(a + 1) * 128],
                                    start=True, stop=True)
                            S["t_z7"].inc(inst)


                @blk.scalar
                def _(s, Hh=Hh, bs=bs):
                    for ti in range(8):
                        s.wait_ge(S["v_a2"].sem, bs["v_a2"] + ti + 1)
                        s.activation(scl7[:], m7[:], AF.Ln)
                        S["s_sc7"].inc(s.activation(
                            scl7[:], scl7[:], AF.Exp, scale=-1.0, bias=LN127))

                @blk.vector
                def _(v, Hh=Hh, bs=bs):
                    for ti in range(8):
                        for r in range(4):
                            ri = 4 * ti + r
                            v.wait_ge(S["t_z7"].sem, bs["t_z7"] + ri + 1)
                            S["v_z7"].inc(v.tensor_copy(
                                zb7[:, r * 1024:(r + 1) * 1024],
                                pz[:, (r % 2) * 1024:(r % 2 + 1) * 1024]))
                        v.tensor_add(zc7[:, 0:2048], zb7[:, 0:2048],
                                     zb7[:, 2048:4096])
                        v.tensor_sub(zc7[:, 2048:4096], zb7[:, 0:2048],
                                     zb7[:, 2048:4096])
                        for st in range(1, 5):
                            half = 32 >> (st + 1)
                            grp = 1 << st
                            src = zc7 if st % 2 == 1 else zb7
                            dst = zb7 if st % 2 == 1 else zc7
                            blkw = 32 // grp
                            dims = [[4096, 128], [blkw * 128, grp],
                                    [1, half * 128]]
                            v.tensor_add(A(dst, 0, dims), A(src, 0, dims),
                                         A(src, half * 128, dims))
                            v.tensor_sub(A(dst, half * 128, dims),
                                         A(src, 0, dims),
                                         A(src, half * 128, dims))
                        zf = zc7
                        if ti >= 1:
                            v.wait_ge(S["s_sc7"].sem, bs["s_sc7"] + ti)
                        v.tensor_reduce(
                            m7[:], A(zf, 0, [[4096, 128], [1, 128],
                                             [128, 32]]),
                            AX.X, ALU.max, apply_absolute_value=True)
                        v.tensor_copy(sh7[0:64, :], m7[64:128, :])
                        v.tensor_tensor(m7[0:64, :], m7[0:64, :], sh7[0:64, :], ALU.max)
                        v.tensor_copy(sh7[0:32, :], m7[32:64, :])
                        v.tensor_tensor(m7[0:32, :], m7[0:32, :], sh7[0:32, :], ALU.max)
                        for stp in (16, 8, 4, 2, 1):
                            v.stream_shuffle(sh7[0:32, :], m7[0:32, :],
                                             [(i2 + stp) % 32 for i2 in range(32)])
                            v.tensor_tensor(m7[0:32, :], m7[0:32, :], sh7[0:32, :], ALU.max)
                        v.tensor_copy(m7[32:64, :], m7[0:32, :])
                        v.tensor_copy(m7[64:96, :], m7[0:32, :])
                        v.tensor_copy(m7[96:128, :], m7[0:32, :])
                        tl = Hh * 8 + ti
                        v.tensor_scalar(m7[:], m7[:], 1e-30, None, ALU.max)
                        S["v_a2"].inc(v.tensor_copy(
                            a2r[0:1, tl * 128:(tl + 1) * 128], m7[0:1, :]))
                        v.wait_ge(S["s_sc7"].sem, bs["s_sc7"] + ti + 1)
                        v.tensor_mul(ru7[:, 0:128], m7[:], scl7[:])
                        v.tensor_scalar(ru7[:, 0:128], ru7[:, 0:128],
                                        -1.0 / 127.0, 2.0, ALU.mult, ALU.add)
                        v.tensor_mul(scl7[:], scl7[:], ru7[:, 0:128])
                        for sub in range(16):
                            xs = A(zf, sub * 2 * 128, [[4096, 128], [128, 2],
                                                       [1, 128]])
                            v.tensor_tensor(ru7[:], xs,
                                            A(scl7, 0, [[128, 128], [0, 2],
                                                        [1, 128]]), ALU.mult)
                            v.tensor_scalar(rv7[:].bitcast(F32), ru7[:],
                                            384.0, None, ALU.add)
                            v.tensor_scalar(rv7[:], rv7[:], FMASK, None,
                                            ALU.bitwise_and)
                            v.tensor_scalar(rt7[:], rv7[:].bitcast(F32),
                                            -383.5, None, ALU.add)
                            v.tensor_tensor(rg7[:], ru7[:], rt7[:], ALU.is_ge)
                            inst = v.scalar_tensor_tensor(
                                A(q2b, sub * 2 * 1024 + ti * 128,
                                  [[32 * 1024, 128], [1024, 2], [1, 128]]),
                                rt7[:], -0.5, rg7[:], ALU.add, ALU.add)
                        S["v_q2"].inc(inst)

            with nc.Block() as blk:
                @blk.vector
                def _(v, Hh=Hh):
                    sl = a1r[0:1, Hh * 1024:(Hh + 1) * 1024]
                    v.tensor_mul(sl, sl, sl)
                    v.tensor_mul(sl, sl,
                                 a2r[0:1, Hh * 1024:(Hh + 1) * 1024])
                    v.tensor_scalar(sl, sl, tsc[0:1, 8:9], None, ALU.mult)
            with nc.Block() as blk:
                @blk.scalar
                def _(s, Hh=Hh):
                    sem_sp.dma(s.dma_start(
                        out=A(g_scr, Hh * 1024, [[1, 1024]]),
                        in_=a1r[0:1, Hh * 1024:(Hh + 1) * 1024]))
                    s.wait_ge(sem_sp.sem, sem_sp.v)
            with nc.Block() as blk:
                @blk.sync
                def _(sync, Hh=Hh):
                    sem_d.dma(sync.dma_start(
                        out=gcols[:],
                        in_=A(g_scr, Hh * 1024, [[1, 128], [128, 8]]),
                        allow_slow_non_contiguous=True))
                    sync.wait_ge(sem_d.sem, sem_d.v)

            for Ch in range(2):
                bs = snap()
                bd = sem_d.v
                if Hh == 0:
                    with nc.Block() as blk:
                        @blk.sync
                        def _(sync, Ch=Ch, bs=bs):
                            for a in range(32):
                                if a >= 1:
                                    sync.wait_ge(S["v_t7"].sem,
                                                 bs["v_t7"] + a)
                                sem_d.dma(sync.dma_start(
                                    out=wst7[:],
                                    in_=A(wdnt, a * 128 * C + Ch * 1024,
                                          [[C, 128], [1, 1024]])))

                        @blk.scalar
                        def _(s, Ch=Ch, bs=bs):
                            for a in range(32):
                                s.wait_ge(S["v_t7"].sem, bs["v_t7"] + a + 1)
                                sem_sp.dma(s.dma_start(
                                    out=A(wdn_scr, a * 128 * C + Ch * 1024,
                                          [[C, 128], [1, 1024]]),
                                    in_=wdnb[:, a * 1024:(a + 1) * 1024]))
                            wdn_tok[Ch] = sem_sp.v

                        @blk.vector
                        def _(v, Ch=Ch, bs=bs, bd=bd):
                            for a in range(32):
                                v.wait_ge(sem_d.sem, bd + 16 * (a + 1))
                                for sb2 in range(2):
                                    ws = wst7[:, sb2 * 512:
                                              (sb2 + 1) * 512]
                                    v.tensor_scalar(tg7p[:], ws, thr[:, 2:3],
                                                    None, ALU.is_ge)
                                    v.tensor_scalar(tg7n[:], ws, thr[:, 3:4],
                                                    None, ALU.is_lt)
                                    inst = v.tensor_sub(
                                        wdnb[:, a * 1024 + sb2 * 512:
                                             a * 1024 + (sb2 + 1) * 512],
                                        tg7p[:], tg7n[:])
                                S["v_t7"].inc(inst)

                else:
                    with nc.Block() as blk:
                        @blk.sync
                        def _(sync, Ch=Ch):
                            sync.wait_ge(sem_sp.sem, wdn_tok[Ch])
                            for a in range(32):
                                sem_d.dma(sync.dma_start(
                                    out=wdnb[:, a * 1024:(a + 1) * 1024],
                                    in_=A(wdn_scr, a * 128 * C + Ch * 1024,
                                          [[C, 128], [1, 1024]])))
                            sync.wait_ge(sem_d.sem, sem_d.v)

                bs = snap()
                bd = sem_d.v
                with nc.Block() as blk:
                    @blk.sync
                    def _(sync, Hh=Hh, Ch=Ch, bs=bs):
                        for ti in range(8):
                            tl = Hh * 8 + ti
                            sync.wait_ge(sem_sp.sem, x2_tok[tl])
                            if ti >= 2:
                                sync.wait_ge(S["v_oe"].sem,
                                             bs["v_oe"] + ti - 1)
                            sem_d.dma(sync.dma_start(
                                out=x2r[:, (ti % 2) * 1024:
                                        (ti % 2 + 1) * 1024],
                                in_=A(x2_scr, tl * 128 * C + Ch * 1024,
                                      [[C, 128], [1, 1024]])))

                    @blk.tensor
                    def _(t, Hh=Hh, bs=bs):
                        for ti in range(8):
                            if ti >= 2:
                                t.wait_ge(S["s_de"].sem, bs["s_de"] + ti - 1)
                            for a in range(32):
                                for n in range(2):
                                    inst = t.matmul(
                                        pz[:, (ti % 2) * 1024 + n * 512:
                                           (ti % 2) * 1024 + (n + 1) * 512],
                                        q2b[:, a * 1024 + ti * 128:
                                            a * 1024 + (ti + 1) * 128],
                                        wdnb[:, a * 1024 + n * 512:
                                             a * 1024 + (n + 1) * 512],
                                        start=(a == 0), stop=(a == 31))
                            S["t_dm"].inc(inst)

                    @blk.scalar
                    def _(s, bs=bs):
                        for ti in range(8):
                            s.wait_ge(S["t_dm"].sem, bs["t_dm"] + ti + 1)
                            if ti >= 2:
                                s.wait_ge(S["v_oe"].sem, bs["v_oe"] + ti - 1)
                            S["s_de"].inc(s.activation(
                                ofn[:, (ti % 2) * 1024:(ti % 2 + 1) * 1024],
                                pz[:, (ti % 2) * 1024:(ti % 2 + 1) * 1024],
                                AF.Identity, scale=gcols[:, ti:ti + 1]))
                            s.wait_ge(S["v_oe"].sem, bs["v_oe"] + ti + 1)
                            tl2 = (bs, ti)
                            sem_o.dma(s.dma_start(
                                out=A(out_own,
                                      (Hh * 8 + ti) * 128 * C + Ch * 1024,
                                      [[C, 128], [1, 1024]]),
                                in_=ofn[:, (ti % 2) * 1024:
                                        (ti % 2 + 1) * 1024]))
                            out_tok[Hh * 8 + ti] = sem_o.v

                    @blk.vector
                    def _(v, Hh=Hh, Ch=Ch, bs=bs, bd=bd):
                        for ti in range(8):
                            tl = Hh * 8 + ti
                            v.wait_ge(S["s_de"].sem, bs["s_de"] + ti + 1)
                            v.wait_ge(sem_d.sem, bd + 16 * (ti + 1))
                            if ti >= 2:
                                v.wait_ge(sem_o.sem, out_tok[tl - 2])
                            S["v_oe"].inc(v.tensor_add(
                                ofn[:, (ti % 2) * 1024:(ti % 2 + 1) * 1024],
                                ofn[:, (ti % 2) * 1024:(ti % 2 + 1) * 1024],
                                x2r[:, (ti % 2) * 1024:(ti % 2 + 1) * 1024]))
        if "q2b" in tap_aps:
            with nc.Block() as blk:
                @blk.sync
                def _(sync):
                    sem_o.dma(sync.dma_start(out=tap_aps["q2b"], in_=q2b[:]))
                    sync.wait_ge(sem_o.sem, sem_o.v)
        if "hs" in tap_aps:
            with nc.Block() as blk:
                @blk.sync
                def _(sync):
                    sem_o.dma(sync.dma_start(out=tap_aps["hs"], in_=h_scr))
                    sync.wait_ge(sem_o.sem, sem_o.v)
        p7.close()

        if "a1r" in tap_aps:
            with nc.Block() as blk:
                @blk.sync
                def _(sync):
                    sem_o.dma(sync.dma_start(out=tap_aps["a1r"], in_=a1r[:]))
                    sync.wait_ge(sem_o.sem, sem_o.v)

        a1r_ctx.close()

        with nc.Block() as blk:
            @blk.sync
            def _(sync):
                sync.wait_ge(sem_o.sem, sem_o.v)
                sync.wait_ge(sem_sp.sem, sem_sp.v)
    return nc



def _rope_tables():
    inv = 1.0 / (10000.0 ** (np.arange(0, HD, 2, dtype=np.float64) / HD))
    pos = np.arange(T, dtype=np.float64)
    ang = pos[None, :] * inv[:, None]
    cos = np.cos(ang)
    sin = np.sin(ang)
    cosr = np.concatenate([cos, cos], axis=0).astype(np.float32)
    sinr = np.concatenate([-sin, sin], axis=0).astype(np.float32)
    return cosr, sinr


def _consts():
    n = 128
    hh = np.array([[1.0]])
    while hh.shape[0] < n:
        hh = np.block([[hh, hh], [hh, -hh]])
    h = hh.astype(ml_dtypes.bfloat16)
    idn = np.eye(128, dtype=ml_dtypes.bfloat16)
    hot16 = np.zeros((128, 256), dtype=ml_dtypes.bfloat16)
    for hd in range(16):
        hot16[:, hd * 16 + hd] = 1
    hot4 = np.zeros((128, 16), dtype=ml_dtypes.bfloat16)
    for g in range(4):
        hot4[:, g * 4 + g] = 1
    rsel4 = np.zeros((4, 512), dtype=np.float32)
    for g in range(4):
        rsel4[g, g * 128:(g + 1) * 128] = 1
    rsel16 = np.zeros((16, 2048), dtype=np.float32)
    for hd in range(16):
        rsel16[hd, hd * 128:(hd + 1) * 128] = 1
    return h, idn, hot16, hot4, rsel4, rsel16


def _edge_masks(parity):
    m = np.zeros((128, 8, 512), dtype=np.float32)
    tri = (np.arange(128)[:, None] <= np.arange(128)[None, :]).astype(
        np.float32)
    for e in range(8):
        for k in range(4):
            d = 2 * k + parity
            if e < d:
                m[:, e, k * 128:(k + 1) * 128] = 1.0
            elif e == d:
                m[:, e, k * 128:(k + 1) * 128] = tri
    return m.reshape(128, 8 * 512).astype(ml_dtypes.bfloat16)


def _perm(n_per_head, total):
    p = np.arange(total).reshape(-1, n_per_head)
    within = np.concatenate([np.arange(0, n_per_head, 2),
                             np.arange(1, n_per_head, 2)])
    return (p[:, within]).reshape(-1)


def kernel(L0_attn_x, L0_attn_Wq, L0_attn_bq, L0_attn_Wk, L0_attn_bk,
           L0_attn_Wv, L0_attn_bv, L0_attn_Wo, L0_attn_bo, L0_attn_alpha,
           L0_ffn_w_up, L0_ffn_w_down):
    x = np.ascontiguousarray(np.asarray(L0_attn_x, dtype=np.float32))
    perm_q = _perm(HD, C)
    perm_k = _perm(HD, KV)
    wq_p = np.ascontiguousarray(np.asarray(L0_attn_Wq,
                                           dtype=np.float32)[:, perm_q])
    bq_p = np.ascontiguousarray(np.asarray(L0_attn_bq,
                                           dtype=np.float32)[perm_q])
    wk_p = np.ascontiguousarray(np.asarray(L0_attn_Wk,
                                           dtype=np.float32)[:, perm_k])
    bk_p = np.ascontiguousarray(np.asarray(L0_attn_bk,
                                           dtype=np.float32)[perm_k])
    wv = np.ascontiguousarray(np.asarray(L0_attn_Wv, dtype=np.float32))
    bv = np.ascontiguousarray(np.asarray(L0_attn_bv, dtype=np.float32))
    wo = np.ascontiguousarray(np.asarray(L0_attn_Wo, dtype=np.float32))
    bo = np.ascontiguousarray(np.asarray(L0_attn_bo, dtype=np.float32))
    al = np.ascontiguousarray(np.asarray(L0_attn_alpha, dtype=np.float32))
    wupt = np.ascontiguousarray(np.asarray(L0_ffn_w_up, dtype=np.float32).T)
    wdnt = np.ascontiguousarray(np.asarray(L0_ffn_w_down, dtype=np.float32).T)
    cosr, sinr = _rope_tables()
    h, idn, hot16, hot4, rsel4, rsel16 = _consts()

    if "nc" not in _CACHE:
        _CACHE["nc"] = build_program()
    nc = _CACHE["nc"]

    in_maps = []
    for core in range(8):
        b = core // 2
        p = core % 2
        xs = x[b]
        xo = np.ascontiguousarray(
            xs.reshape(NSB, 128, C)[p::2].reshape(TOK, C))
        qc = np.ascontiguousarray(
            cosr.reshape(128, NSB, 128)[:, p::2].reshape(128, TOK))
        qs = np.ascontiguousarray(
            sinr.reshape(128, NSB, 128)[:, p::2].reshape(128, TOK))
        in_maps.append(dict(
            x_seq=xs, x_own=xo, wq=wq_p, bq=bq_p, wk=wk_p, bk=bk_p,
            wv=wv, bv=bv, wo=wo, bo=bo, alpha=al, wupt=wupt, wdnt=wdnt,
            cosr=cosr, sinr=sinr, qcos=qc, qsin=qs,
            emasks=_edge_masks(p), h128=h, iden=idn,
            hot16=hot16, hot4=hot4, rsel4=rsel4, rsel16=rsel16))

    res = run_bass_kernel_spmd(nc, in_maps, core_ids=list(range(8)))
    out = np.empty((B, T, C), dtype=np.float32)
    for core in range(8):
        b = core // 2
        p = core % 2
        oo = res.results[core]["out_own"].reshape(NT, 128, C)
        out[b].reshape(NSB, 128, C)[p::2] = oo
    return out
